# version 11
# baseline (speedup 1.0000x reference)
import numpy as np
import ml_dtypes
import concourse.bass as bass
import concourse.mybir as mybir
from concourse.ap import AP
from concourse.bass_utils import run_bass_kernel_spmd

F32 = mybir.dt.float32
BF16 = mybir.dt.bfloat16
I32 = mybir.dt.int32
AF = mybir.ActivationFunctionType
ALU = mybir.AluOpType
AX = mybir.AxisListType

B, S, D = 4, 4096, 1024
H, DK = 16, 64
LH = 8
FF = 2816
TOK = S // 2
NQC = S // 512
NKT = S // 128
GW = 1280
MW = 1152
NG = 4
GT = TOK // NG
EPS = 1e-5
KB = 1024
NCH = [(0, 512), (512, 512), (1024, 512), (1536, 512), (2048, 512), (2560, 256)]


def _bucket_host(r):
    import math
    n = -r
    ret = 16 if n < 0 else 0
    n = abs(n)
    if n < 8:
        return ret + n
    v = 8 + int(math.log(max(n, 1) / 8.0) / math.log(128.0 / 8.0) * 8.0)
    return ret + min(v, 15)


def _onehot_host():
    oh = np.zeros((32, GW), dtype=np.float32)
    for u in range(GW):
        oh[_bucket_host(u - 640), u] = 1.0
    return oh


def _batches():
    out, kt = [], 0
    while kt < NKT:
        n = min(3, NKT - kt)
        out.append(list(range(kt, kt + n)))
        kt += n
    return out


def _tile_class(qc, kt):
    d = 128 * kt - 512 * qc
    if -218 <= d <= 602:
        return "band"
    return "fut" if d > 0 else "past"


def build_nc():
    nc = bass.Bass("TRN2", target_bir_lowering=False)

    xb = nc.dram_tensor("xb", [S, D], F32, kind="ExternalInput").ap()
    xres = nc.dram_tensor("xres", [TOK, D], F32, kind="ExternalInput").ap()
    wqi = nc.dram_tensor("wqi", [D, LH * DK], F32, kind="ExternalInput").ap()
    wki = nc.dram_tensor("wki", [D, LH * DK], F32, kind="ExternalInput").ap()
    wvi = nc.dram_tensor("wvi", [D, LH * DK], F32, kind="ExternalInput").ap()
    woi = nc.dram_tensor("woi", [D, D], F32, kind="ExternalInput").ap()
    reli = nc.dram_tensor("reli", [32, LH], F32, kind="ExternalInput").ap()
    wgi = nc.dram_tensor("wgi", [FF, D], F32, kind="ExternalInput").ap()
    wui = nc.dram_tensor("wui", [FF, D], F32, kind="ExternalInput").ap()
    wdi = nc.dram_tensor("wdi", [D, FF], F32, kind="ExternalInput").ap()
    onehot = nc.dram_tensor("onehot", [32, GW], F32, kind="ExternalInput").ap()
    ones128 = nc.dram_tensor("ones128", [1, 128], BF16, kind="ExternalInput").ap()
    identb = nc.dram_tensor("identb", [128, 128], BF16, kind="ExternalInput").ap()
    thoff = nc.dram_tensor("thoff", [1, 1], I32, kind="ExternalInput").ap()
    out = nc.dram_tensor("out", [TOK, D], F32, kind="ExternalOutput").ap()

    xbf = nc.dram_tensor("xbf", [S, D], BF16).ap()
    gbd = nc.dram_tensor("gbd", [LH, GW], BF16).ap()
    glogd = nc.dram_tensor("glogd", [LH, GW], BF16).ap()
    drowd = nc.dram_tensor("drowd", [LH, 512], BF16).ap()
    rmd = nc.dram_tensor("rmd", [LH * 128, MW], BF16).ap()
    ddram = nc.dram_tensor("ddram", [LH, 1], F32).ap()
    recd = nc.dram_tensor("recd", [2, 512], BF16).ap()
    ctxh = nc.dram_tensor("ctxh", [LH * DK, S], BF16).ap()
    ctxall = nc.dram_tensor("ctxall", [H * DK, S], BF16).ap()
    xqd = nc.dram_tensor("xqd", [GT, D], BF16).ap()
    hqd = nc.dram_tensor("hqd", [GT, FF], BF16).ap()
    wq_d = {"wg": nc.dram_tensor("wgq", [FF, D], BF16).ap(),
            "wu": nc.dram_tensor("wuq", [FF, D], BF16).ap(),
            "wd": nc.dram_tensor("wdq", [D, FF], BF16).ap()}
    wsrc = {"wg": wgi, "wu": wui, "wd": wdi}

    base = (nc.sbuf_base + 31) // 32 * 32

    def at(name, off_b, shape, dtype):
        return nc.alloc_sbuf_tensor_at(name, shape, dtype, offset=base + off_b)

    XT = [at(f"xt{k}", 8 * KB * k, [128, S], BF16) for k in range(8)]
    oh_sb = at("oh_sb", 0, [32, GW], F32)
    z1g = at("z1g", 0, [128, 4096], F32)
    xqg = at("xqg", 16 * KB, [128, 4096], BF16)
    q_yn = at("q_yn", 24 * KB, [128, 2816], F32)
    q_s1 = at("q_s1", 35 * KB, [128, 2816], BF16)
    q_s2 = at("q_s2", 41 * KB, [128, 2816], BF16)
    hf32 = at("hf32", 47 * KB, [128, 512], F32)
    gsb2 = at("gsb2", 49 * KB, [128, 512], F32)
    KT = [at(f"kt{p}", 64 * KB + 8 * KB * p, [128, S], BF16) for p in range(4)]
    ctxT = [at(f"ctxt{k}", 64 * KB + 4 * KB * k, [128, TOK], BF16) for k in range(8)]
    VS = [at(f"vs{t}", 96 * KB + 1088 * t, [128, LH * 65], BF16) for t in range(32)]
    wstage = at("wstage", 96 * KB, [128, 2816], F32)
    wqout = at("wqout", 107 * KB, [128, 2816], BF16)
    hbuf = at("hbuf", 96 * KB, [128, 4 * FF], BF16)
    outst = at("outst", 119 * KB, [128, 2048], F32)
    QT = [at(f"qt{p}", 130 * KB + 8 * KB * p, [128, S], BF16) for p in range(4)]
    bvsb = at("bvsb", 130 * KB, [LH, GW], F32)
    bv2sb = at("bv2sb", 136 * KB, [LH, GW], F32)
    gbsb = at("gbsb", 142 * KB, [LH, GW], BF16)
    wo_sb = at("wo_sb", 130 * KB, [128, 8 * KB], BF16)
    wtile = [at(f"wt{i}", 146 * KB + KB * i, [128, 512], BF16) for i in range(16)]
    qkvw = at("qkvw", 162 * KB, [128, 3 * 8 * 512], BF16)
    P0 = [at(f"p0r{i}", 162 * KB + 3 * KB * i, [128, 1536], BF16) for i in range(2)]
    P1 = [at(f"p1r{i}", 168 * KB + 3 * KB * i, [128, 1536], BF16) for i in range(2)]
    RM2 = [at(f"rm2_{e}", 186 * KB + 2304 * e, [128, MW], BF16) for e in range(2)]
    drow2 = [at(f"drow2_{e}", 190 * KB + 512 + 1024 * e, [128, 512], BF16) for e in range(2)]
    ident_sb = at("ident_sb", 192 * KB + 768, [128, 128], BF16)
    m2tmp = [at(f"m2t_{e}", 2304 * e, [128, MW], BF16) for e in range(8)]
    rmsb = [at(f"rms_{e}", 18 * KB + 2304 * e, [128, MW], BF16) for e in range(8)]
    xqT = [at(f"xqt{k}", 162 * KB + KB * k, [128, GT], BF16) for k in range(8)]
    hqT = [at(f"hqt{k}", 170 * KB + KB * k, [128, GT], BF16) for k in range(22)]
    FB = 194 * KB
    rel_sb = at("rel_sb", FB + 0, [32, LH], F32)
    ones_sb = at("ones_sb", FB + 64, [1, 128], BF16)
    thoff_sb = at("thoff_sb", FB + 352, [1, 1], I32)
    dcol = at("dcol", FB + 384, [128, LH], F32)
    edcol = at("edcol", FB + 416, [128, LH], F32)
    stats = at("stats", FB + 448, [128, 12], F32)
    aggr = at("aggr", FB + 512, [128, 8], F32)
    rstd = at("rstd", FB + 544, [128, 4], F32)
    lnt = at("lnt", FB + 576, [128, 4], F32)
    epscol = at("epscol", FB + 608, [128, 1], F32)
    lnr = at("lnr", FB + 640, [1, 512], F32)
    recb = [at("rec0b", FB + 2688, [1, 512], BF16),
            at("rec1b", FB + 3712, [1, 512], BF16)]
    ctmp = [at("ctmp0", FB + 4736, [64, 512], BF16),
            at("ctmp1", FB + 5760, [64, 512], BF16)]
    q_m = at("q_m", FB + 6784, [128, 384], F32)
    q_r = at("q_r", FB + 8320, [128, 384], F32)
    q_sh = at("q_sh", FB + 9856, [128, 384], BF16)
    rbuf2 = [at("rbufa", FB + 10624, [64, 512], BF16),
             at("rbufb", FB + 11648, [64, 512], BF16)]
    bvlog = at("bvlog", 36 * KB, [LH, GW], BF16)
    drowsb = at("drowsb", 39 * KB, [LH, 512], BF16)
    gsb = at("gsb", 49 * KB, [128, 512], BF16)
    assert base + FB + 12672 <= nc.sbuf_top, "sbuf overflow"

    Sq0 = nc.alloc_psum_tensor("Sq0", [128, 1536], F32)
    Sq1 = nc.alloc_psum_tensor("Sq1", [128, 1536], F32)
    C0 = nc.alloc_psum_tensor("C0", [65, 512], F32)
    C1 = nc.alloc_psum_tensor("C1", [65, 512], F32)
    PS = [Sq0[:, 0:512], Sq0[:, 512:1024], Sq0[:, 1024:1536],
          Sq1[:, 0:512], Sq1[:, 512:1024], Sq1[:, 1024:1536]]
    Sq = [Sq0, Sq1]
    Ca = [C0, C1]
    Pr = [P0, P1]

    semnames = ("ld0 xbfs xts qws bmm b1 b2 b3 mstd qmm qev s0 s1 e0 e1 m0 m1 "
                "pv0 pv1 rc rb ct cv ctxo cc ctxi wos vself sself "
                "wsts gq wqs xrs wmm wev lnv lns xqs xqdd xqts wts gmm sil gev "
                "hqv hqdd hqts dws dmm dev outs obar glg m2t rms tdl drw").split()
    sems = {n: nc.ctx.enter_context(nc.semaphore(n)) for n in semnames}

    batches = _batches()
    NB = len(batches)
    NPQ = 4 * NQC

    def batch_class(qc, bi):
        cls = [_tile_class(qc, kt) for kt in batches[bi]]
        if all(c == "past" for c in cls):
            return "past"
        if all(c == "fut" for c in cls):
            return "fut"
        return "mixed"

    SCH = {"sync": [], "gpsimd": [], "vector": [], "scalar": [], "tensor": []}

    def emit(e, fn, inc=None, n=1):
        def go(eng, fn=fn, inc=inc, n=n):
            ins = fn(eng)
            if inc is not None:
                ins.then_inc(sems[inc], n)
        SCH[e].append(go)

    def wait(e, sem, val):
        if val > 0:
            SCH[e].append(lambda eng, s=sem, x=val: eng.wait_ge(sems[s], x))

    vself = [0]
    sself = [0]

    def vput(fn, inc=None):
        vself[0] += 1
        n = vself[0]

        def go(v, fn=fn, n=n, inc=inc):
            fn(v).then_inc(sems["vself"], 1)
            v.wait_ge(sems["vself"], n)
            if inc is not None:
                v.engine_nop().then_inc(sems[inc], 1)
        SCH["vector"].append(go)

    def sput(fn, inc=None):
        sself[0] += 1
        n = sself[0]

        def go(s, fn=fn, n=n, inc=inc):
            fn(s).then_inc(sems["sself"], 1)
            s.wait_ge(sems["sself"], n)
            if inc is not None:
                s.nop().then_inc(sems[inc], 1)
        SCH["scalar"].append(go)

    def quant_ladder(src, outap, F, inc=None):
        nb = F // 16
        src3 = src.rearrange("p (n c) -> p n c", c=16)
        vput(lambda v: v.tensor_reduce(out=q_m[:, 0:nb], in_=src3, axis=AX.X,
                                       op=ALU.max, apply_absolute_value=True))
        vput(lambda v: v.tensor_scalar(out=q_m[:, 0:nb], in0=q_m[:, 0:nb],
                                       scalar1=1.0 / 6.0, scalar2=1e-12,
                                       op0=ALU.mult, op1=ALU.max))
        vput(lambda v: v.reciprocal(out=q_r[:, 0:nb], in_=q_m[:, 0:nb]))
        vput(lambda v: v.tensor_scalar(out=q_sh[:, 0:nb], in0=q_m[:, 0:nb],
                                       scalar1=0.5, scalar2=None, op0=ALU.mult))
        vput(lambda v: v.scalar_tensor_tensor(out=q_yn[:, 0:F], in0=src,
                                              scalar=-1.0, in1=src,
                                              op0=ALU.mult, op1=ALU.max))
        r3 = q_r[:, 0:nb].unsqueeze(2).broadcast_to([128, nb, 16])
        yn3 = q_yn[:, 0:F].rearrange("p (n c) -> p n c", c=16)
        vput(lambda v: v.tensor_tensor(out=yn3, in0=yn3, in1=r3, op=ALU.mult))
        vput(lambda v: v.tensor_scalar(out=q_s1[:, 0:F], in0=q_yn[:, 0:F],
                                       scalar1=0.25, scalar2=None, op0=ALU.is_gt))
        for th in (0.75, 1.25, 1.75):
            vput(lambda v, th=th: v.scalar_tensor_tensor(
                out=q_s1[:, 0:F], in0=q_yn[:, 0:F], scalar=th, in1=q_s1[:, 0:F],
                op0=ALU.is_gt, op1=ALU.add))
        vput(lambda v: v.tensor_scalar(out=q_s2[:, 0:F], in0=q_yn[:, 0:F],
                                       scalar1=2.5, scalar2=None, op0=ALU.is_gt))
        for th in (3.5, 5.0, 5.0):
            vput(lambda v, th=th: v.scalar_tensor_tensor(
                out=q_s2[:, 0:F], in0=q_yn[:, 0:F], scalar=th, in1=q_s2[:, 0:F],
                op0=ALU.is_gt, op1=ALU.add))
        vput(lambda v: v.scalar_tensor_tensor(out=q_s1[:, 0:F], in0=q_s2[:, 0:F],
                                              scalar=2.0, in1=q_s1[:, 0:F],
                                              op0=ALU.mult, op1=ALU.add))
        vput(lambda v: v.tensor_scalar(out=q_s2[:, 0:F], in0=src,
                                       scalar1=0.0, scalar2=2.0,
                                       op0=ALU.is_gt, op1=ALU.mult))
        vput(lambda v: v.tensor_scalar(out=q_s2[:, 0:F], in0=q_s2[:, 0:F],
                                       scalar1=1.0, scalar2=None, op0=ALU.subtract))
        sh3 = q_sh[:, 0:nb].unsqueeze(2).broadcast_to([128, nb, 16])
        s13 = q_s1[:, 0:F].rearrange("p (n c) -> p n c", c=16)
        vput(lambda v: v.tensor_tensor(out=s13, in0=s13, in1=sh3, op=ALU.mult))
        vput(lambda v: v.tensor_tensor(out=outap, in0=q_s1[:, 0:F],
                                       in1=q_s2[:, 0:F], op=ALU.mult), inc=inc)

    def ln_stats(src1024, t):
        vput(lambda v: v.bn_stats(out=stats[:, 0:6], in_=src1024[:, 0:512]))
        vput(lambda v: v.bn_stats(out=stats[:, 6:12], in_=src1024[:, 512:1024]))
        vput(lambda v: v.bn_aggr(out=aggr[:, 2 * t:2 * t + 2], in_=stats[:, :]))

    def ln_rstd(lns_val):
        va = aggr[:, :].rearrange("p (t two) -> p t two", two=2)[:, :, 1:2]
        lo = lnt[:, :].rearrange("p (t o) -> p t o", o=1)
        ro = rstd[:, :].rearrange("p (t o) -> p t o", o=1)
        sput(lambda s: s.activation(out=lo, in_=va, func=AF.Ln, bias=epscol[:, :]))
        sput(lambda s: s.activation(out=ro, in_=lo, func=AF.Exp, scale=-0.5),
             inc=lns_val)

    emit("sync", lambda y: y.dma_start(out=oh_sb[:, :], in_=onehot), "ld0", 16)
    emit("sync", lambda y: y.dma_start(out=rel_sb[:, :], in_=reli), "ld0", 16)
    emit("sync", lambda y: y.dma_start(out=ones_sb[:, :], in_=ones128), "ld0", 16)
    emit("sync", lambda y: y.dma_start(out=ident_sb[:, :], in_=identb), "ld0", 16)
    emit("sync", lambda y: y.dma_start(out=thoff_sb[:, :], in_=thoff), "ld0", 16)
    for i in range(4):
        emit("gpsimd", lambda g, i=i: g.dma_start(
            out=xbf[i * 1024:(i + 1) * 1024, :],
            in_=xb[i * 1024:(i + 1) * 1024, :]), "xbfs", 16)
    for m, w in enumerate((wqi, wki, wvi)):
        emit("gpsimd", lambda g, m=m, w=w: g.dma_start(
            out=qkvw[:, m * 8 * 512:(m + 1) * 8 * 512].rearrange(
                "p (kc n) -> p kc n", n=512),
            in_=w.rearrange("(kc p) n -> p kc n", p=128)), "qws", 16)

    wait("tensor", "ld0", 32)
    for ci, (c0, cn) in enumerate(((0, 512), (512, 512), (1024, 256))):
        emit("tensor", lambda t, c0=c0, cn=cn: t.matmul(
            Sq0[0:LH, c0:c0 + cn], rel_sb[:, :], oh_sb[:, c0:c0 + cn],
            start=True, stop=True), "bmm" if ci == 2 else None)
    vput(lambda v: v.memset(epscol[:, :], EPS))
    wait("vector", "bmm", 1)
    vput(lambda v: v.tensor_copy(out=bvsb[:, :], in_=Sq0[0:LH, 0:GW]))
    vput(lambda v: v.tensor_scalar(out=bv2sb[:, :], in0=bvsb[:, :],
                                   scalar1=bvsb[:, 0:1], scalar2=None,
                                   op0=ALU.subtract), inc="b1")
    vput(lambda v: v.tensor_copy(out=bvlog[:, :], in_=bv2sb[:, :]))
    vput(lambda v: v.tensor_scalar(out=drowsb[:, :], in0=bvsb[:, 0:512],
                                   scalar1=0.0, scalar2=bv2sb[:, GW - 1:GW],
                                   op0=ALU.mult, op1=ALU.add), inc="drw")
    wait("scalar", "b1", 1)
    emit("scalar", lambda s: s.activation(out=gbsb[:, :], in_=bv2sb[:, :],
                                          func=AF.Exp), "b2")
    wait("sync", "b2", 1)
    emit("sync", lambda y: y.dma_start(out=gbd, in_=gbsb[:, :]), "mstd", 16)
    emit("sync", lambda y: y.dma_start(out=ddram, in_=bv2sb[:, GW - 1:GW]), "mstd", 16)
    wait("sync", "mstd", 32)
    emit("sync", lambda y: y.dma_start(
        out=dcol[:, :], in_=AP(ddram.tensor, 0, [[0, 128], [1, LH]])), "mstd", 16)
    wait("sync", "drw", 1)
    emit("sync", lambda y: y.dma_start(out=glogd, in_=bvlog[:, :]), "glg", 16)
    emit("sync", lambda y: y.dma_start(out=drowd, in_=drowsb[:, :]), "glg", 16)
    wait("sync", "glg", 32)
    for lh in range(LH):
        emit("sync", lambda y, lh=lh: y.dma_start(
            out=m2tmp[lh][:, :],
            in_=AP(glogd.tensor, lh * GW + 1, [[1, 128], [1, MW]])), "m2t", 16)
    wait("vector", "m2t", 16 * LH)
    for lh in range(LH):
        pstr = m2tmp[lh].ap().ap[0][0]
        rev = AP(m2tmp[lh].ap().tensor, MW - 1, [[pstr, 128], [-1, MW]])
        vput(lambda v, lh=lh, rev=rev: v.tensor_copy(out=rmsb[lh][:, :], in_=rev),
             inc="rms" if lh == LH - 1 else None)
    wait("sync", "rms", 1)
    for lh in range(LH):
        emit("sync", lambda y, lh=lh: y.dma_start(
            out=rmd[lh * 128:(lh + 1) * 128, :], in_=rmsb[lh][:, :]), "rms", 16)

    wait("sync", "xbfs", 64)
    wait("sync", "bmm", 1)
    wait("sync", "rms", 1 + 16 * LH)
    for k in range(8):
        emit("sync", lambda y, k=k: y.dma_start_transpose(
            XT[k][:, :], xbf[:, k * 128:(k + 1) * 128]), "xts", 16)

    wait("tensor", "xts", 128)
    wait("tensor", "qws", 48)
    wait("tensor", "b1", 1)
    for t in range(32):
        vput(lambda v, t=t: v.memset(
            VS[t][:, :].rearrange("p (h c) -> p h c", c=65)[:, :, 64:65], 1.0))
    wait("vector", "mstd", 32)
    qkv_rounds = ([("q", ot, nq) for ot in range(4) for nq in range(8)] +
                  [("k", ot, nq) for ot in range(4) for nq in range(8)] +
                  [("v", tt, 0) for tt in range(32)])
    for idx, (kind, a, nq) in enumerate(qkv_rounds):
        ps = PS[idx % 6]
        wait("tensor", "qev", idx)
        for kc in range(8):
            if kind == "q":
                lhsT = qkvw[:, kc * 512 + a * 128:kc * 512 + a * 128 + 128]
                rhs = XT[kc][:, nq * 512:(nq + 1) * 512]
            elif kind == "k":
                lhsT = qkvw[:, (8 + kc) * 512 + a * 128:(8 + kc) * 512 + a * 128 + 128]
                rhs = XT[kc][:, nq * 512:(nq + 1) * 512]
            else:
                lhsT = XT[kc][:, a * 128:(a + 1) * 128]
                rhs = qkvw[:, (16 + kc) * 512:(16 + kc) * 512 + 512]
            emit("tensor", lambda t, l=lhsT, r=rhs, ps=ps, kc=kc: t.matmul(
                ps, l, r, start=(kc == 0), stop=(kc == 7)),
                "qmm" if kc == 7 else None)
        wait("vector", "qmm", idx + 1)
        if kind == "q":
            dst, psv = QT[a][:, nq * 512:(nq + 1) * 512], ps
        elif kind == "k":
            dst, psv = KT[a][:, nq * 512:(nq + 1) * 512], ps
        else:
            dst = VS[a][:, :].rearrange("p (h c) -> p h c", c=65)[:, :, 0:64]
            psv = ps.rearrange("p (h c) -> p h c", c=64)
        vput(lambda v, dst=dst, psv=psv: v.tensor_copy(out=dst, in_=psv),
             inc="qev")
    NQKV = len(qkv_rounds)

    MSTD_ALL = 48

    wait("scalar", "mstd", 48)
    wait("tensor", "qev", NQKV)
    sS, sE, sM, sPV = ["s0", "s1"], ["e0", "e1"], ["m0", "m1"], ["pv0", "pv1"]
    for p in range(4):
        wait("sync", "s0", p * NQC * NB)
        wait("sync", "s1", p * NQC * NB)
        for e in range(2):
            lh = 2 * p + e
            emit("sync", lambda y, e=e, lh=lh: y.dma_start(
                out=RM2[e][:, :], in_=rmd[lh * 128:(lh + 1) * 128, :]), "tdl", 16)
            emit("sync", lambda y, e=e, lh=lh: y.dma_start(
                out=drow2[e][:, :],
                in_=AP(drowd.tensor, lh * 512, [[0, 128], [1, 512]])), "tdl", 16)
        wait("tensor", "tdl", 64 * (p + 1))
        for qc in range(NQC):
            ridx = p * NQC + qc
            G0 = ridx * NB
            def emit_scores(bi):
                G = G0 + bi
                bk = batches[bi]
                cls = batch_class(qc, bi)
                for e in range(2):
                    wait("tensor", sE[e], G)
                for si, kt in enumerate(bk):
                    tc = _tile_class(qc, kt)
                    extra = (cls == "mixed") and (tc != "past")
                    for e in range(2):
                        lhsT = KT[p][e * 64:(e + 1) * 64, kt * 128:(kt + 1) * 128]
                        rhs = QT[p][e * 64:(e + 1) * 64, qc * 512:(qc + 1) * 512]
                        last = (si == len(bk) - 1) and not extra
                        emit("tensor", lambda t, l=lhsT, r=rhs, e=e, si=si, extra=extra:
                             t.matmul(Sq[e][:, si * 512:(si + 1) * 512], l, r,
                                      start=True, stop=not extra,
                                      tile_position=(64 * e, 0)),
                             sS[e] if last else None)
                    if extra:
                        d = 128 * kt - 512 * qc
                        for e in range(2):
                            last = (si == len(bk) - 1)
                            if tc == "band":
                                u0 = 512 - d
                                emit("tensor", lambda t, e=e, si=si, u0=u0: t.matmul(
                                    Sq[e][:, si * 512:(si + 1) * 512], ident_sb[:, :],
                                    RM2[e][:, u0:u0 + 512], start=False, stop=True),
                                    sS[e] if last else None)
                            else:
                                emit("tensor", lambda t, e=e, si=si: t.matmul(
                                    Sq[e][:, si * 512:(si + 1) * 512], ident_sb[:, :],
                                    drow2[e][:, :], start=False, stop=True),
                                    sS[e] if last else None)

            def emit_pv(bi):
                G = G0 + bi
                bk = batches[bi]
                if bi == 0:
                    wait("tensor", "cv", ridx)
                for e in range(2):
                    lh = 2 * p + e
                    wait("tensor", sM[e], G + 1)
                    for si, kt in enumerate(bk):
                        st = (bi == 0 and si == 0)
                        sp = (bi == NB - 1 and si == len(bk) - 1)
                        emit("tensor",
                             lambda t, e=e, kt=kt, si=si, st=st, sp=sp, G=G, lh=lh:
                             t.matmul(Ca[e][:, :],
                                      VS[kt][:, lh * 65:lh * 65 + 65],
                                      Pr[e][G % 2][:, si * 512:(si + 1) * 512],
                                      start=st, stop=sp),
                             sPV[e] if si == len(bk) - 1 else None)

            wait("tensor", "rc", 2 * ridx)
            wait("tensor", "ct", 2 * ridx)
            for bi in range(NB):
                G = G0 + bi
                bk = batches[bi]
                FD = 512 * len(bk)
                cls = batch_class(qc, bi)
                if bi > 0:
                    emit_pv(bi - 1)
                emit_scores(bi)
                for e in range(2):
                    lh = 2 * p + e
                    wait("scalar", sS[e], G + 1)
                    wait("scalar", sPV[e], G - 1)
                    bias = dcol[:, lh:lh + 1] if cls == "fut" else 0.0
                    emit("scalar", lambda s, e=e, FD=FD, G=G, bias=bias: s.activation(
                        out=Pr[e][G % 2][:, 0:FD], in_=Sq[e][:, 0:FD],
                        func=AF.Exp, bias=bias), sE[e])
                for e in range(2):
                    wait("vector", sE[e], G + 1)
                    emit("vector", lambda v: v.engine_nop(), sM[e])
            emit_pv(NB - 1)
            GL = G0 + NB
            wait("scalar", "rb", 64 * ridx)
            for e in range(2):
                wait("scalar", sPV[e], GL)
                sput(lambda s, e=e: s.activation(out=lnr[:, :],
                                                 in_=Ca[e][64:65, :], func=AF.Ln))
                sput(lambda s, e=e: s.activation(out=recb[e][:, :],
                                                 in_=lnr[:, :], func=AF.Exp,
                                                 scale=-1.0), inc="rc")
            wait("sync", "cv", ridx)
            for e in range(2):
                wait("sync", "rc", 2 * ridx + e + 1)
                emit("sync", lambda y, e=e: y.dma_start(
                    out=recd[e:e + 1, :], in_=recb[e][:, :]), "rb", 16)
                wait("sync", "rb", 64 * ridx + 32 * e + 16)
                emit("sync", lambda y, e=e: y.dma_start(
                    out=rbuf2[e][:, :],
                    in_=AP(recd.tensor, e * 512, [[0, 64], [1, 512]])), "rb", 16)
            for e in range(2):
                wait("vector", sPV[e], GL)
                vput(lambda v, e=e: v.tensor_copy(out=ctmp[e][:, :],
                                                  in_=Ca[e][0:64, :]), inc="ct")
            wait("vector", "ct", 2 * ridx + 2)
            wait("vector", "rb", 64 * ridx + 64)
            for e in range(2):
                vput(lambda v, e=e, p=p, qc=qc: v.tensor_tensor(
                    out=QT[p][e * 64:(e + 1) * 64, qc * 512:(qc + 1) * 512],
                    in0=ctmp[e][:, :], in1=rbuf2[e][:, :],
                    op=ALU.mult), inc="cv" if e == 1 else None)

    wait("sync", "cv", NPQ)
    for p in range(4):
        emit("sync", lambda y, p=p: y.dma_start(
            out=ctxh[p * 128:(p + 1) * 128, :], in_=QT[p][:, :]), "ctxo", 16)
    wait("gpsimd", "ctxo", 64)
    emit("gpsimd", lambda g: g.collective_compute(
        "AllGather", ALU.bypass,
        replica_groups=[[0, 1], [2, 3], [4, 5], [6, 7]],
        ins=[ctxh.opt()], outs=[ctxall.opt()]), "cc")
    emit("gpsimd", lambda g: g.dma_start(
        out=wo_sb[:, :].rearrange("p (kc n) -> p kc n", n=1024),
        in_=woi.rearrange("(kc p) n -> p kc n", p=128)), "wos", 16)
    wait("sync", "cc", 1)

    def ctx_loads(y):
        roff = y.value_load(thoff_sb[0:1, 0:1])
        for k in range(8):
            src = AP(ctxall.tensor, (k * 128) * S + roff, [[S, 128], [1, TOK]])
            y.dma_start(out=ctxT[k][:, :], in_=src).then_inc(sems["ctxi"], 16)
    SCH["sync"].append(ctx_loads)

    WQG = ([("wg", r0, 256) for r0 in range(0, FF, 256)] +
           [("wu", r0, 256) for r0 in range(0, FF, 256)] +
           [("wd", r0, 128) for r0 in range(0, D, 128)])
    NWQ = len(WQG)
    for gi, (mat, r0, n) in enumerate(WQG):
        width = D if mat in ("wg", "wu") else FF
        F = (n // 128) * width
        wait("gpsimd", "gq", gi)
        if gi == 0:
            wait("gpsimd", "pv0", NPQ * NB)
            wait("gpsimd", "pv1", NPQ * NB)
        emit("gpsimd", lambda g, mat=mat, r0=r0, n=n, F=F, width=width: g.dma_start(
            out=wstage[:, 0:F].rearrange("p (t d) -> p t d", d=width),
            in_=wsrc[mat][r0:r0 + n, :].rearrange("(t p) d -> p t d", p=128)),
            "wsts", 16)
        wait("vector", "wsts", 16 * (gi + 1))
        wait("vector", "wqs", 16 * gi)
        quant_ladder(wstage[:, 0:F], wqout[:, 0:F], F, inc="gq")
        wait("gpsimd", "gq", gi + 1)
        emit("gpsimd", lambda g, mat=mat, r0=r0, n=n, F=F, width=width: g.dma_start(
            out=wq_d[mat][r0:r0 + n, :].rearrange("(t p) d -> p t d", p=128),
            in_=wqout[:, 0:F].rearrange("p (t d) -> p t d", d=width)), "wqs", 16)

    wait("tensor", "ctxi", 128)
    wait("tensor", "wos", 16)
    wait("tensor", "cv", NPQ)
    widx = [0]
    gcnt = [0]
    dcnt = [0]
    decnt = [0]
    wtcnt = [0]
    dwcnt = [0]
    hqcnt = [0]
    xqtcnt = [0]
    lncnt = [0]
    lscnt = [0]
    for g in range(NG):
        wait("sync", "outs", 2 * g)
        emit("sync", lambda y, g=g: y.dma_start(
            out=z1g[:, :].rearrange("p (t d) -> p t d", d=D),
            in_=xres[g * GT:(g + 1) * GT, :].rearrange("(t p) d -> p t d", p=128)),
            "xrs", 16)
        wait("tensor", "dev", decnt[0])
        for tt in range(4):
            for nch2 in range(2):
                i = widx[0]
                widx[0] += 1
                ps = PS[i % 6]
                wait("tensor", "wev", i)
                for kc in range(8):
                    lhsT = ctxT[kc][:, (g * 4 + tt) * 128:(g * 4 + tt) * 128 + 128]
                    rhs = wo_sb[:, kc * 1024 + nch2 * 512:kc * 1024 + nch2 * 512 + 512]
                    emit("tensor", lambda t, l=lhsT, r=rhs, ps=ps, kc=kc: t.matmul(
                        ps, l, r, start=(kc == 0), stop=(kc == 7)),
                        "wmm" if kc == 7 else None)
                wait("vector", "wmm", i + 1)
                wait("vector", "xrs", 16 * (g + 1))
                dst = z1g[:, tt * 1024 + nch2 * 512:tt * 1024 + nch2 * 512 + 512]
                vput(lambda v, dst=dst, ps=ps: v.tensor_tensor(
                    out=dst, in0=dst, in1=ps, op=ALU.add), inc="wev")
        for tt in range(4):
            ln_stats(z1g[:, tt * 1024:(tt + 1) * 1024], tt)
        lncnt[0] += 1
        SCH["vector"].append(lambda v: v.engine_nop().then_inc(sems["lnv"], 1))
        wait("scalar", "lnv", lncnt[0])
        lscnt[0] += 1
        ln_rstd("lns")
        wait("vector", "lns", lscnt[0])
        for tt in range(4):
            dst = z1g[:, tt * 1024:(tt + 1) * 1024]
            vput(lambda v, dst=dst, tt=tt: v.tensor_scalar(
                out=dst, in0=dst, scalar1=aggr[:, 2 * tt:2 * tt + 1],
                scalar2=rstd[:, tt:tt + 1], op0=ALU.subtract, op1=ALU.mult))
        wait("vector", "xqdd", 16 * g)
        quant_ladder(z1g[:, 0:2048], xqg[:, 0:2048], 2048)
        quant_ladder(z1g[:, 2048:4096], xqg[:, 2048:4096], 2048, inc="xqs")
        wait("sync", "xqs", g + 1)
        emit("sync", lambda y: y.dma_start(
            out=xqd[:, :].rearrange("(t p) d -> p t d", p=128),
            in_=xqg[:, :].rearrange("p (t d) -> p t d", d=D)), "xqdd", 16)
        wait("sync", "xqdd", 16 * (g + 1))
        wait("sync", "gmm", gcnt[0])
        for k in range(8):
            emit("sync", lambda y, k=k: y.dma_start_transpose(
                xqT[k][:, :], xqd[:, k * 128:(k + 1) * 128]), "xqts", 16)
        xqtcnt[0] += 16 * 8
        wait("tensor", "xqts", xqtcnt[0])
        if g == 0:
            wait("sync", "wqs", 16 * NWQ)
            wait("tensor", "gev", 0)
        for ci, (n0, n) in enumerate(NCH):
            wait("sync", "gev", gcnt[0])
            wait("sync", "dmm", dcnt[0])
            for kc in range(8):
                emit("sync", lambda y, kc=kc, n0=n0, n=n: y.dma_start_transpose(
                    wtile[kc][:, 0:n],
                    wq_d["wg"][n0:n0 + n, kc * 128:(kc + 1) * 128]), "wts", 16)
                emit("sync", lambda y, kc=kc, n0=n0, n=n: y.dma_start_transpose(
                    wtile[8 + kc][:, 0:n],
                    wq_d["wu"][n0:n0 + n, kc * 128:(kc + 1) * 128]), "wts", 16)
            wtcnt[0] += 16 * 16
            wait("tensor", "wts", wtcnt[0])
            for tt in range(4):
                gps = PS[2 * (tt % 2)]
                ups = PS[2 * (tt % 2) + 1]
                wait("tensor", "gev", gcnt[0])
                for kc in range(8):
                    lhsT = xqT[kc][:, tt * 128:(tt + 1) * 128]
                    emit("tensor", lambda t, l=lhsT, kc=kc, gps=gps, n=n: t.matmul(
                        gps[:, 0:n], l, wtile[kc][:, 0:n],
                        start=(kc == 0), stop=(kc == 7)), None)
                    emit("tensor", lambda t, l=lhsT, kc=kc, ups=ups, n=n: t.matmul(
                        ups[:, 0:n], l, wtile[8 + kc][:, 0:n],
                        start=(kc == 0), stop=(kc == 7)),
                        "gmm" if kc == 7 else None)
                gcnt[0] += 1
                wait("scalar", "gmm", gcnt[0])
                wait("scalar", "gev", gcnt[0] - 1)
                emit("scalar", lambda s, gps=gps, n=n: s.activation(
                    out=gsb2[:, 0:n], in_=gps[:, 0:n], func=AF.Silu), "sil")
                wait("vector", "sil", gcnt[0])
                wait("vector", "gmm", gcnt[0])
                if ci == 0 and tt == 0:
                    wait("vector", "hqdd", 16 * g)
                vput(lambda v, gps=gps, ups=ups, n=n: v.tensor_tensor(
                    out=hf32[:, 0:n], in0=gsb2[:, 0:n], in1=ups[:, 0:n],
                    op=ALU.mult), inc="gev")
                dst = hbuf[:, tt * FF + n0:tt * FF + n0 + n]
                last_ld = (ci == len(NCH) - 1 and tt == 3)
                quant_ladder(hf32[:, 0:n], dst, n,
                             inc="hqv" if last_ld else None)
        wait("sync", "hqv", g + 1)
        emit("sync", lambda y: y.dma_start(
            out=hqd[:, :].rearrange("(t p) d -> p t d", p=128),
            in_=hbuf[:, :].rearrange("p (t d) -> p t d", d=FF)), "hqdd", 16)
        wait("sync", "hqdd", 16 * (g + 1))
        wait("sync", "dmm", dcnt[0])
        for kc in range(22):
            emit("sync", lambda y, kc=kc: y.dma_start_transpose(
                hqT[kc][:, :], hqd[:, kc * 128:(kc + 1) * 128]), "hqts", 16)
        hqcnt[0] += 16 * 22
        wait("tensor", "hqts", hqcnt[0])
        wait("sync", "gmm", gcnt[0])
        wait("tensor", "gev", gcnt[0])
        for ci2 in range(2):
            wait("tensor", "dev", decnt[0])
            for kc in range(22):
                wsl = kc % 4
                wait("sync", "dmm", dcnt[0] - 3)
                emit("sync", lambda y, kc=kc, ci2=ci2, wsl=wsl: y.dma_start_transpose(
                    wtile[wsl][:, 0:512],
                    wq_d["wd"][ci2 * 512:ci2 * 512 + 512,
                               kc * 128:(kc + 1) * 128]), "dws", 16)
                dwcnt[0] += 16
                wait("tensor", "dws", dwcnt[0])
                for tt in range(4):
                    lhsT = hqT[kc][:, tt * 128:(tt + 1) * 128]
                    emit("tensor", lambda t, l=lhsT, wsl=wsl, tt=tt, kc=kc: t.matmul(
                        PS[tt], l, wtile[wsl][:, 0:512],
                        start=(kc == 0), stop=(kc == 21)),
                        "dmm" if tt == 3 else None)
                dcnt[0] += 1
            wait("vector", "dmm", dcnt[0])
            for tt in range(4):
                dst = z1g[:, tt * 1024 + ci2 * 512:tt * 1024 + ci2 * 512 + 512]
                vput(lambda v, dst=dst, tt=tt: v.tensor_tensor(
                    out=dst, in0=dst, in1=PS[tt], op=ALU.add),
                    inc="dev" if tt == 3 else None)
            decnt[0] += 1
        for tt in range(4):
            ln_stats(z1g[:, tt * 1024:(tt + 1) * 1024], tt)
        lncnt[0] += 1
        SCH["vector"].append(lambda v: v.engine_nop().then_inc(sems["lnv"], 1))
        wait("scalar", "lnv", lncnt[0])
        lscnt[0] += 1
        ln_rstd("lns")
        wait("vector", "lns", lscnt[0])
        for half in range(2):
            wait("vector", "obar", 16 * (2 * g + half))
            for tt2 in range(2):
                tt = 2 * half + tt2
                src = z1g[:, tt * 1024:(tt + 1) * 1024]
                vput(lambda v, src=src, tt=tt, tt2=tt2: v.tensor_scalar(
                    out=outst[:, tt2 * 1024:(tt2 + 1) * 1024], in0=src,
                    scalar1=aggr[:, 2 * tt:2 * tt + 1], scalar2=rstd[:, tt:tt + 1],
                    op0=ALU.subtract, op1=ALU.mult),
                    inc="outs" if tt2 == 1 else None)
            wait("sync", "outs", 2 * g + half + 1)
            emit("sync", lambda y, g=g, half=half: y.dma_start(
                out=out[g * GT + half * 256:g * GT + (half + 1) * 256, :]
                .rearrange("(t p) d -> p t d", p=128),
                in_=outst[:, :].rearrange("p (t d) -> p t d", d=D)), "obar", 16)
    wait("sync", "obar", 16 * 2 * NG)

    with nc.Block() as block:
        @block.sync
        def _(y):
            for fn in SCH["sync"]:
                fn(y)

        @block.gpsimd
        def _(g):
            for fn in SCH["gpsimd"]:
                fn(g)

        @block.vector
        def _(v):
            for fn in SCH["vector"]:
                fn(v)

        @block.scalar
        def _(s):
            for fn in SCH["scalar"]:
                fn(s)

        @block.tensor
        def _(t):
            for fn in SCH["tensor"]:
                fn(t)

    return nc


_NC_CACHE = [None]


def kernel(**inputs):
    x = np.ascontiguousarray(np.asarray(inputs["L0_attn_x"], dtype=np.float32))
    Wq = np.ascontiguousarray(np.asarray(inputs["L0_attn_Wq"], dtype=np.float32))
    Wk = np.ascontiguousarray(np.asarray(inputs["L0_attn_Wk"], dtype=np.float32))
    Wv = np.ascontiguousarray(np.asarray(inputs["L0_attn_Wv"], dtype=np.float32))
    Wo = np.ascontiguousarray(np.asarray(inputs["L0_attn_Wo"], dtype=np.float32))
    rel = np.ascontiguousarray(np.asarray(inputs["L0_attn_rel_emb"], dtype=np.float32))
    wg = np.ascontiguousarray(np.asarray(inputs["L0_ffn_w_gate"], dtype=np.float32))
    wu = np.ascontiguousarray(np.asarray(inputs["L0_ffn_w_up"], dtype=np.float32))
    wd = np.ascontiguousarray(np.asarray(inputs["L0_ffn_w_down"], dtype=np.float32))

    if _NC_CACHE[0] is None:
        _NC_CACHE[0] = build_nc()
    nc = _NC_CACHE[0]

    oh = _onehot_host()
    o128 = np.ones((1, 128), dtype=ml_dtypes.bfloat16)
    idb = np.eye(128, dtype=np.float32).astype(ml_dtypes.bfloat16)
    in_maps = []
    for c in range(8):
        b, hh = c // 2, c % 2
        in_maps.append(dict(
            xb=x[b],
            xres=np.ascontiguousarray(x[b, hh * TOK:(hh + 1) * TOK]),
            wqi=np.ascontiguousarray(Wq[:, hh * 512:(hh + 1) * 512]),
            wki=np.ascontiguousarray(Wk[:, hh * 512:(hh + 1) * 512]),
            wvi=np.ascontiguousarray(Wv[:, hh * 512:(hh + 1) * 512]),
            woi=Wo,
            reli=np.ascontiguousarray(rel[:, hh * LH:(hh + 1) * LH]),
            wgi=wg, wui=wu, wdi=wd,
            onehot=oh, ones128=o128, identb=idb,
            thoff=np.array([[hh * TOK]], dtype=np.int32),
        ))
    res = run_bass_kernel_spmd(nc, in_maps, core_ids=list(range(8)))
    outf = np.empty((B, S, D), dtype=np.float32)
    for c in range(8):
        b, hh = c // 2, c % 2
        outf[b, hh * TOK:(hh + 1) * TOK] = np.asarray(res.results[c]["out"])
    return outf


# version 12 (final)
# speedup vs baseline: 1.0049x; 1.0049x over previous
import numpy as np
import ml_dtypes
import concourse.bass as bass
import concourse.mybir as mybir
from concourse.ap import AP
from concourse.bass_utils import run_bass_kernel_spmd

F32 = mybir.dt.float32
BF16 = mybir.dt.bfloat16
I32 = mybir.dt.int32
AF = mybir.ActivationFunctionType
ALU = mybir.AluOpType
AX = mybir.AxisListType

B, S, D = 4, 4096, 1024
H, DK = 16, 64
LH = 8
FF = 2816
TOK = S // 2
NQC = S // 512
NKT = S // 128
GW = 1280
MW = 1152
NG = 4
GT = TOK // NG
EPS = 1e-5
KB = 1024
NCH = [(0, 512), (512, 512), (1024, 512), (1536, 512), (2048, 512), (2560, 256)]


def _bucket_host(r):
    import math
    n = -r
    ret = 16 if n < 0 else 0
    n = abs(n)
    if n < 8:
        return ret + n
    v = 8 + int(math.log(max(n, 1) / 8.0) / math.log(128.0 / 8.0) * 8.0)
    return ret + min(v, 15)


def _onehot_host():
    oh = np.zeros((32, GW), dtype=np.float32)
    for u in range(GW):
        oh[_bucket_host(u - 640), u] = 1.0
    return oh


def _batches():
    out, kt = [], 0
    while kt < NKT:
        n = min(3, NKT - kt)
        out.append(list(range(kt, kt + n)))
        kt += n
    return out


def _tile_class(qc, kt):
    d = 128 * kt - 512 * qc
    if -218 <= d <= 602:
        return "band"
    return "fut" if d > 0 else "past"


def build_nc():
    nc = bass.Bass("TRN2", target_bir_lowering=False)

    xb = nc.dram_tensor("xb", [S, D], F32, kind="ExternalInput").ap()
    xres = nc.dram_tensor("xres", [TOK, D], F32, kind="ExternalInput").ap()
    wqi = nc.dram_tensor("wqi", [D, LH * DK], F32, kind="ExternalInput").ap()
    wki = nc.dram_tensor("wki", [D, LH * DK], F32, kind="ExternalInput").ap()
    wvi = nc.dram_tensor("wvi", [D, LH * DK], F32, kind="ExternalInput").ap()
    woi = nc.dram_tensor("woi", [D, D], F32, kind="ExternalInput").ap()
    reli = nc.dram_tensor("reli", [32, LH], F32, kind="ExternalInput").ap()
    wgi = nc.dram_tensor("wgi", [FF, D], F32, kind="ExternalInput").ap()
    wui = nc.dram_tensor("wui", [FF, D], F32, kind="ExternalInput").ap()
    wdi = nc.dram_tensor("wdi", [D, FF], F32, kind="ExternalInput").ap()
    onehot = nc.dram_tensor("onehot", [32, GW], F32, kind="ExternalInput").ap()
    ones128 = nc.dram_tensor("ones128", [1, 128], BF16, kind="ExternalInput").ap()
    identb = nc.dram_tensor("identb", [128, 128], BF16, kind="ExternalInput").ap()
    thoff = nc.dram_tensor("thoff", [1, 1], I32, kind="ExternalInput").ap()
    out = nc.dram_tensor("out", [TOK, D], F32, kind="ExternalOutput").ap()

    xbf = nc.dram_tensor("xbf", [S, D], BF16).ap()
    gbd = nc.dram_tensor("gbd", [LH, GW], BF16).ap()
    glogd = nc.dram_tensor("glogd", [LH, GW], BF16).ap()
    drowd = nc.dram_tensor("drowd", [LH, 512], BF16).ap()
    rmd = nc.dram_tensor("rmd", [LH * 128, MW], BF16).ap()
    ddram = nc.dram_tensor("ddram", [LH, 1], F32).ap()
    recd = nc.dram_tensor("recd", [2, 512], BF16).ap()
    ctxh = nc.dram_tensor("ctxh", [LH * DK, S], BF16).ap()
    ctxall = nc.dram_tensor("ctxall", [H * DK, S], BF16).ap()
    xqd = nc.dram_tensor("xqd", [GT, D], BF16).ap()
    hqd = nc.dram_tensor("hqd", [GT, FF], BF16).ap()
    wq_d = {"wg": nc.dram_tensor("wgq", [FF, D], BF16).ap(),
            "wu": nc.dram_tensor("wuq", [FF, D], BF16).ap(),
            "wd": nc.dram_tensor("wdq", [D, FF], BF16).ap()}
    wsrc = {"wg": wgi, "wu": wui, "wd": wdi}

    base = (nc.sbuf_base + 31) // 32 * 32

    def at(name, off_b, shape, dtype):
        return nc.alloc_sbuf_tensor_at(name, shape, dtype, offset=base + off_b)

    XT = [at(f"xt{k}", 8 * KB * k, [128, S], BF16) for k in range(8)]
    oh_sb = at("oh_sb", 0, [32, GW], F32)
    z1g = at("z1g", 0, [128, 4096], F32)
    xqg = at("xqg", 16 * KB, [128, 4096], BF16)
    q_yn = at("q_yn", 24 * KB, [128, 2816], F32)
    q_s1 = at("q_s1", 35 * KB, [128, 2816], BF16)
    q_s2 = at("q_s2", 41 * KB, [128, 2816], BF16)
    hf32 = at("hf32", 47 * KB, [128, 512], F32)
    gsb2 = at("gsb2", 49 * KB, [128, 512], F32)
    KT = [at(f"kt{p}", 64 * KB + 8 * KB * p, [128, S], BF16) for p in range(4)]
    ctxT = [at(f"ctxt{k}", 64 * KB + 4 * KB * k, [128, TOK], BF16) for k in range(8)]
    VS = [at(f"vs{t}", 96 * KB + 1088 * t, [128, LH * 65], BF16) for t in range(32)]
    wstage = at("wstage", 96 * KB, [128, 2816], F32)
    wqout = at("wqout", 107 * KB, [128, 2816], BF16)
    hbuf = at("hbuf", 96 * KB, [128, 4 * FF], BF16)
    outst = at("outst", 119 * KB, [128, 2048], F32)
    QT = [at(f"qt{p}", 130 * KB + 8 * KB * p, [128, S], BF16) for p in range(4)]
    bvsb = at("bvsb", 130 * KB, [LH, GW], F32)
    bv2sb = at("bv2sb", 136 * KB, [LH, GW], F32)
    gbsb = at("gbsb", 142 * KB, [LH, GW], BF16)
    wo_sb = at("wo_sb", 130 * KB, [128, 8 * KB], BF16)
    wtile = [at(f"wt{i}", 146 * KB + KB * i, [128, 512], BF16) for i in range(16)]
    qkvw = at("qkvw", 162 * KB, [128, 3 * 8 * 512], BF16)
    P0 = [at(f"p0r{i}", 162 * KB + 3 * KB * i, [128, 1536], BF16) for i in range(2)]
    P1 = [at(f"p1r{i}", 168 * KB + 3 * KB * i, [128, 1536], BF16) for i in range(2)]
    RM2 = [at(f"rm2_{e}", 186 * KB + 2304 * e, [128, MW], BF16) for e in range(2)]
    drow2 = [at(f"drow2_{e}", 190 * KB + 512 + 1024 * e, [128, 512], BF16) for e in range(2)]
    ident_sb = at("ident_sb", 192 * KB + 768, [128, 128], BF16)
    m2tmp = [at(f"m2t_{e}", 2304 * e, [128, MW], BF16) for e in range(8)]
    rmsb = [at(f"rms_{e}", 18 * KB + 2304 * e, [128, MW], BF16) for e in range(8)]
    xqT = [at(f"xqt{k}", 162 * KB + KB * k, [128, GT], BF16) for k in range(8)]
    hqT = [at(f"hqt{k}", 170 * KB + KB * k, [128, GT], BF16) for k in range(22)]
    FB = 194 * KB
    rel_sb = at("rel_sb", FB + 0, [32, LH], F32)
    ones_sb = at("ones_sb", FB + 64, [1, 128], BF16)
    thoff_sb = at("thoff_sb", FB + 352, [1, 1], I32)
    dcol = at("dcol", FB + 384, [128, LH], F32)
    edcol = at("edcol", FB + 416, [128, LH], F32)
    stats = at("stats", FB + 448, [128, 12], F32)
    aggr = at("aggr", FB + 512, [128, 8], F32)
    rstd = at("rstd", FB + 544, [128, 4], F32)
    lnt = at("lnt", FB + 576, [128, 4], F32)
    epscol = at("epscol", FB + 608, [128, 1], F32)
    lnr = at("lnr", FB + 640, [1, 512], F32)
    recb = [at("rec0b", FB + 2688, [1, 512], BF16),
            at("rec1b", FB + 3712, [1, 512], BF16)]
    ctmp = [at("ctmp0", FB + 4736, [64, 512], BF16),
            at("ctmp1", FB + 5760, [64, 512], BF16)]
    q_m = at("q_m", FB + 6784, [128, 384], F32)
    q_r = at("q_r", FB + 8320, [128, 384], F32)
    q_sh = at("q_sh", FB + 9856, [128, 384], BF16)
    rbuf2 = [at("rbufa", FB + 10624, [64, 512], BF16),
             at("rbufb", FB + 11648, [64, 512], BF16)]
    bvlog = at("bvlog", 36 * KB, [LH, GW], BF16)
    drowsb = at("drowsb", 39 * KB, [LH, 512], BF16)
    gsb = at("gsb", 49 * KB, [128, 512], BF16)
    assert base + FB + 12672 <= nc.sbuf_top, "sbuf overflow"

    Sq0 = nc.alloc_psum_tensor("Sq0", [128, 1536], F32)
    Sq1 = nc.alloc_psum_tensor("Sq1", [128, 1536], F32)
    C0 = nc.alloc_psum_tensor("C0", [65, 512], F32)
    C1 = nc.alloc_psum_tensor("C1", [65, 512], F32)
    PS = [Sq0[:, 0:512], Sq0[:, 512:1024], Sq0[:, 1024:1536],
          Sq1[:, 0:512], Sq1[:, 512:1024], Sq1[:, 1024:1536]]
    Sq = [Sq0, Sq1]
    Ca = [C0, C1]
    Pr = [P0, P1]

    semnames = ("ld0 xbfs xts qws bmm b1 b2 b3 mstd qmm qev s0 s1 e0 e1 m0 m1 "
                "pv0 pv1 rc rb ct cv ctxo cc ctxi wos vself sself "
                "wsts gq wqs xrs wmm wev lnv lns xqs xqdd xqts wts gmm sil gev "
                "hqv hqdd hqts dws dmm dev outs obar glg m2t rms tdl drw").split()
    sems = {n: nc.ctx.enter_context(nc.semaphore(n)) for n in semnames}

    batches = _batches()
    NB = len(batches)
    NPQ = 4 * NQC

    def batch_class(qc, bi):
        cls = [_tile_class(qc, kt) for kt in batches[bi]]
        if all(c == "past" for c in cls):
            return "past"
        if all(c == "fut" for c in cls):
            return "fut"
        return "mixed"

    SCH = {"sync": [], "gpsimd": [], "vector": [], "scalar": [], "tensor": []}

    def emit(e, fn, inc=None, n=1):
        def go(eng, fn=fn, inc=inc, n=n):
            ins = fn(eng)
            if inc is not None:
                ins.then_inc(sems[inc], n)
        SCH[e].append(go)

    def wait(e, sem, val):
        if val > 0:
            SCH[e].append(lambda eng, s=sem, x=val: eng.wait_ge(sems[s], x))

    vself = [0]
    sself = [0]

    def vput(fn, inc=None):
        vself[0] += 1
        n = vself[0]

        def go(v, fn=fn, n=n, inc=inc):
            fn(v).then_inc(sems["vself"], 1)
            v.wait_ge(sems["vself"], n)
            if inc is not None:
                v.engine_nop().then_inc(sems[inc], 1)
        SCH["vector"].append(go)

    def sput(fn, inc=None):
        sself[0] += 1
        n = sself[0]

        def go(s, fn=fn, n=n, inc=inc):
            fn(s).then_inc(sems["sself"], 1)
            s.wait_ge(sems["sself"], n)
            if inc is not None:
                s.nop().then_inc(sems[inc], 1)
        SCH["scalar"].append(go)

    def quant_ladder(src, outap, F, inc=None):
        nb = F // 16
        src3 = src.rearrange("p (n c) -> p n c", c=16)
        vput(lambda v: v.tensor_reduce(out=q_m[:, 0:nb], in_=src3, axis=AX.X,
                                       op=ALU.max, apply_absolute_value=True))
        vput(lambda v: v.tensor_scalar(out=q_m[:, 0:nb], in0=q_m[:, 0:nb],
                                       scalar1=1.0 / 6.0, scalar2=1e-12,
                                       op0=ALU.mult, op1=ALU.max))
        vput(lambda v: v.reciprocal(out=q_r[:, 0:nb], in_=q_m[:, 0:nb]))
        vput(lambda v: v.tensor_scalar(out=q_sh[:, 0:nb], in0=q_m[:, 0:nb],
                                       scalar1=0.5, scalar2=None, op0=ALU.mult))
        vput(lambda v: v.scalar_tensor_tensor(out=q_yn[:, 0:F], in0=src,
                                              scalar=-1.0, in1=src,
                                              op0=ALU.mult, op1=ALU.max))
        r3 = q_r[:, 0:nb].unsqueeze(2).broadcast_to([128, nb, 16])
        yn3 = q_yn[:, 0:F].rearrange("p (n c) -> p n c", c=16)
        vput(lambda v: v.tensor_tensor(out=yn3, in0=yn3, in1=r3, op=ALU.mult))
        vput(lambda v: v.tensor_scalar(out=q_s1[:, 0:F], in0=q_yn[:, 0:F],
                                       scalar1=0.25, scalar2=None, op0=ALU.is_gt))
        for th in (0.75, 1.25, 1.75):
            vput(lambda v, th=th: v.scalar_tensor_tensor(
                out=q_s1[:, 0:F], in0=q_yn[:, 0:F], scalar=th, in1=q_s1[:, 0:F],
                op0=ALU.is_gt, op1=ALU.add))
        vput(lambda v: v.tensor_scalar(out=q_s2[:, 0:F], in0=q_yn[:, 0:F],
                                       scalar1=2.5, scalar2=None, op0=ALU.is_gt))
        for th in (3.5, 5.0, 5.0):
            vput(lambda v, th=th: v.scalar_tensor_tensor(
                out=q_s2[:, 0:F], in0=q_yn[:, 0:F], scalar=th, in1=q_s2[:, 0:F],
                op0=ALU.is_gt, op1=ALU.add))
        vput(lambda v: v.scalar_tensor_tensor(out=q_s1[:, 0:F], in0=q_s2[:, 0:F],
                                              scalar=2.0, in1=q_s1[:, 0:F],
                                              op0=ALU.mult, op1=ALU.add))
        vput(lambda v: v.tensor_scalar(out=q_s2[:, 0:F], in0=src,
                                       scalar1=0.0, scalar2=2.0,
                                       op0=ALU.is_gt, op1=ALU.mult))
        vput(lambda v: v.tensor_scalar(out=q_s2[:, 0:F], in0=q_s2[:, 0:F],
                                       scalar1=1.0, scalar2=None, op0=ALU.subtract))
        sh3 = q_sh[:, 0:nb].unsqueeze(2).broadcast_to([128, nb, 16])
        s13 = q_s1[:, 0:F].rearrange("p (n c) -> p n c", c=16)
        vput(lambda v: v.tensor_tensor(out=s13, in0=s13, in1=sh3, op=ALU.mult))
        vput(lambda v: v.tensor_tensor(out=outap, in0=q_s1[:, 0:F],
                                       in1=q_s2[:, 0:F], op=ALU.mult), inc=inc)

    def ln_stats(src1024, t):
        vput(lambda v: v.bn_stats(out=stats[:, 0:6], in_=src1024[:, 0:512]))
        vput(lambda v: v.bn_stats(out=stats[:, 6:12], in_=src1024[:, 512:1024]))
        vput(lambda v: v.bn_aggr(out=aggr[:, 2 * t:2 * t + 2], in_=stats[:, :]))

    def ln_rstd(lns_val):
        va = aggr[:, :].rearrange("p (t two) -> p t two", two=2)[:, :, 1:2]
        lo = lnt[:, :].rearrange("p (t o) -> p t o", o=1)
        ro = rstd[:, :].rearrange("p (t o) -> p t o", o=1)
        sput(lambda s: s.activation(out=lo, in_=va, func=AF.Ln, bias=epscol[:, :]))
        sput(lambda s: s.activation(out=ro, in_=lo, func=AF.Exp, scale=-0.5),
             inc=lns_val)

    emit("sync", lambda y: y.dma_start(out=oh_sb[:, :], in_=onehot), "ld0", 16)
    emit("sync", lambda y: y.dma_start(out=rel_sb[:, :], in_=reli), "ld0", 16)
    emit("sync", lambda y: y.dma_start(out=ones_sb[:, :], in_=ones128), "ld0", 16)
    emit("sync", lambda y: y.dma_start(out=ident_sb[:, :], in_=identb), "ld0", 16)
    emit("sync", lambda y: y.dma_start(out=thoff_sb[:, :], in_=thoff), "ld0", 16)
    for i in range(4):
        emit("gpsimd", lambda g, i=i: g.dma_start(
            out=xbf[i * 1024:(i + 1) * 1024, :],
            in_=xb[i * 1024:(i + 1) * 1024, :]), "xbfs", 16)
    for m, w in enumerate((wqi, wki, wvi)):
        emit("gpsimd", lambda g, m=m, w=w: g.dma_start(
            out=qkvw[:, m * 8 * 512:(m + 1) * 8 * 512].rearrange(
                "p (kc n) -> p kc n", n=512),
            in_=w.rearrange("(kc p) n -> p kc n", p=128)), "qws", 16)

    wait("tensor", "ld0", 80)
    for ci, (c0, cn) in enumerate(((0, 512), (512, 512), (1024, 256))):
        emit("tensor", lambda t, c0=c0, cn=cn: t.matmul(
            Sq0[0:LH, c0:c0 + cn], rel_sb[:, :], oh_sb[:, c0:c0 + cn],
            start=True, stop=True), "bmm" if ci == 2 else None)
    vput(lambda v: v.memset(epscol[:, :], EPS))
    wait("vector", "bmm", 1)
    vput(lambda v: v.tensor_copy(out=bvsb[:, :], in_=Sq0[0:LH, 0:GW]))
    vput(lambda v: v.tensor_scalar(out=bv2sb[:, :], in0=bvsb[:, :],
                                   scalar1=bvsb[:, 0:1], scalar2=None,
                                   op0=ALU.subtract), inc="b1")
    vput(lambda v: v.tensor_copy(out=bvlog[:, :], in_=bv2sb[:, :]))
    vput(lambda v: v.tensor_scalar(out=drowsb[:, :], in0=bvsb[:, 0:512],
                                   scalar1=0.0, scalar2=bv2sb[:, GW - 1:GW],
                                   op0=ALU.mult, op1=ALU.add), inc="drw")
    wait("scalar", "b1", 1)
    emit("scalar", lambda s: s.activation(out=gbsb[:, :], in_=bv2sb[:, :],
                                          func=AF.Exp), "b2")
    wait("sync", "b2", 1)
    emit("sync", lambda y: y.dma_start(out=gbd, in_=gbsb[:, :]), "mstd", 16)
    emit("sync", lambda y: y.dma_start(out=ddram, in_=bv2sb[:, GW - 1:GW]), "mstd", 16)
    wait("sync", "mstd", 32)
    emit("sync", lambda y: y.dma_start(
        out=dcol[:, :], in_=AP(ddram.tensor, 0, [[0, 128], [1, LH]])), "mstd", 16)
    wait("sync", "drw", 1)
    emit("sync", lambda y: y.dma_start(out=glogd, in_=bvlog[:, :]), "glg", 16)
    emit("sync", lambda y: y.dma_start(out=drowd, in_=drowsb[:, :]), "glg", 16)
    wait("sync", "glg", 32)
    for lh in range(LH):
        emit("sync", lambda y, lh=lh: y.dma_start(
            out=m2tmp[lh][:, :],
            in_=AP(glogd.tensor, lh * GW + 1, [[1, 128], [1, MW]])), "m2t", 16)
    wait("vector", "m2t", 16 * LH)
    for lh in range(LH):
        pstr = m2tmp[lh].ap().ap[0][0]
        rev = AP(m2tmp[lh].ap().tensor, MW - 1, [[pstr, 128], [-1, MW]])
        vput(lambda v, lh=lh, rev=rev: v.tensor_copy(out=rmsb[lh][:, :], in_=rev),
             inc="rms" if lh == LH - 1 else None)
    wait("sync", "rms", 1)
    for lh in range(LH):
        emit("sync", lambda y, lh=lh: y.dma_start(
            out=rmd[lh * 128:(lh + 1) * 128, :], in_=rmsb[lh][:, :]), "rms", 16)

    wait("sync", "xbfs", 64)
    wait("sync", "bmm", 1)
    wait("sync", "rms", 1 + 16 * LH)
    for k in range(8):
        emit("sync", lambda y, k=k: y.dma_start_transpose(
            XT[k][:, :], xbf[:, k * 128:(k + 1) * 128]), "xts", 16)

    wait("tensor", "xts", 128)
    wait("tensor", "qws", 48)
    wait("tensor", "b1", 1)
    for t in range(32):
        vput(lambda v, t=t: v.memset(
            VS[t][:, :].rearrange("p (h c) -> p h c", c=65)[:, :, 64:65], 1.0))
    wait("vector", "mstd", 32)
    qkv_rounds = ([("q", ot, nq) for ot in range(4) for nq in range(8)] +
                  [("k", ot, nq) for ot in range(4) for nq in range(8)] +
                  [("v", tt, 0) for tt in range(32)])
    for idx, (kind, a, nq) in enumerate(qkv_rounds):
        ps = PS[idx % 6]
        wait("tensor", "qev", idx)
        for kc in range(8):
            if kind == "q":
                lhsT = qkvw[:, kc * 512 + a * 128:kc * 512 + a * 128 + 128]
                rhs = XT[kc][:, nq * 512:(nq + 1) * 512]
            elif kind == "k":
                lhsT = qkvw[:, (8 + kc) * 512 + a * 128:(8 + kc) * 512 + a * 128 + 128]
                rhs = XT[kc][:, nq * 512:(nq + 1) * 512]
            else:
                lhsT = XT[kc][:, a * 128:(a + 1) * 128]
                rhs = qkvw[:, (16 + kc) * 512:(16 + kc) * 512 + 512]
            emit("tensor", lambda t, l=lhsT, r=rhs, ps=ps, kc=kc: t.matmul(
                ps, l, r, start=(kc == 0), stop=(kc == 7)),
                "qmm" if kc == 7 else None)
        wait("vector", "qmm", idx + 1)
        if kind == "q":
            dst, psv = QT[a][:, nq * 512:(nq + 1) * 512], ps
        elif kind == "k":
            dst, psv = KT[a][:, nq * 512:(nq + 1) * 512], ps
        else:
            dst = VS[a][:, :].rearrange("p (h c) -> p h c", c=65)[:, :, 0:64]
            psv = ps.rearrange("p (h c) -> p h c", c=64)
        vput(lambda v, dst=dst, psv=psv: v.tensor_copy(out=dst, in_=psv),
             inc="qev")
    NQKV = len(qkv_rounds)

    MSTD_ALL = 48

    wait("scalar", "mstd", 48)
    wait("tensor", "qev", NQKV)
    sS, sE, sM, sPV = ["s0", "s1"], ["e0", "e1"], ["m0", "m1"], ["pv0", "pv1"]
    for p in range(4):
        wait("sync", "s0", p * NQC * NB)
        wait("sync", "s1", p * NQC * NB)
        for e in range(2):
            lh = 2 * p + e
            emit("sync", lambda y, e=e, lh=lh: y.dma_start(
                out=RM2[e][:, :], in_=rmd[lh * 128:(lh + 1) * 128, :]), "tdl", 16)
            emit("sync", lambda y, e=e, lh=lh: y.dma_start(
                out=drow2[e][:, :],
                in_=AP(drowd.tensor, lh * 512, [[0, 128], [1, 512]])), "tdl", 16)
        wait("tensor", "tdl", 64 * (p + 1))
        for qc in range(NQC):
            ridx = p * NQC + qc
            G0 = ridx * NB
            def emit_scores(bi):
                G = G0 + bi
                bk = batches[bi]
                cls = batch_class(qc, bi)
                for e in range(2):
                    wait("tensor", sE[e], G)
                for si, kt in enumerate(bk):
                    tc = _tile_class(qc, kt)
                    extra = (cls == "mixed") and (tc != "past")
                    for e in range(2):
                        lhsT = KT[p][e * 64:(e + 1) * 64, kt * 128:(kt + 1) * 128]
                        rhs = QT[p][e * 64:(e + 1) * 64, qc * 512:(qc + 1) * 512]
                        last = (si == len(bk) - 1) and not extra
                        emit("tensor", lambda t, l=lhsT, r=rhs, e=e, si=si, extra=extra:
                             t.matmul(Sq[e][:, si * 512:(si + 1) * 512], l, r,
                                      start=True, stop=not extra,
                                      tile_position=(64 * e, 0)),
                             sS[e] if last else None)
                    if extra:
                        d = 128 * kt - 512 * qc
                        for e in range(2):
                            last = (si == len(bk) - 1)
                            if tc == "band":
                                u0 = 512 - d
                                emit("tensor", lambda t, e=e, si=si, u0=u0: t.matmul(
                                    Sq[e][:, si * 512:(si + 1) * 512], ident_sb[:, :],
                                    RM2[e][:, u0:u0 + 512], start=False, stop=True),
                                    sS[e] if last else None)
                            else:
                                emit("tensor", lambda t, e=e, si=si: t.matmul(
                                    Sq[e][:, si * 512:(si + 1) * 512], ident_sb[:, :],
                                    drow2[e][:, :], start=False, stop=True),
                                    sS[e] if last else None)

            def emit_pv(bi):
                G = G0 + bi
                bk = batches[bi]
                if bi == 0:
                    wait("tensor", "cv", ridx)
                for e in range(2):
                    lh = 2 * p + e
                    wait("tensor", sM[e], G + 1)
                    for si, kt in enumerate(bk):
                        st = (bi == 0 and si == 0)
                        sp = (bi == NB - 1 and si == len(bk) - 1)
                        emit("tensor",
                             lambda t, e=e, kt=kt, si=si, st=st, sp=sp, G=G, lh=lh:
                             t.matmul(Ca[e][:, :],
                                      VS[kt][:, lh * 65:lh * 65 + 65],
                                      Pr[e][G % 2][:, si * 512:(si + 1) * 512],
                                      start=st, stop=sp),
                             sPV[e] if si == len(bk) - 1 else None)

            wait("tensor", "rc", 2 * ridx)
            wait("tensor", "ct", 2 * ridx)
            for bi in range(NB):
                G = G0 + bi
                bk = batches[bi]
                FD = 512 * len(bk)
                cls = batch_class(qc, bi)
                if bi > 0:
                    emit_pv(bi - 1)
                emit_scores(bi)
                for e in range(2):
                    lh = 2 * p + e
                    wait("scalar", sS[e], G + 1)
                    wait("scalar", sPV[e], G - 1)
                    bias = dcol[:, lh:lh + 1] if cls == "fut" else 0.0
                    emit("scalar", lambda s, e=e, FD=FD, G=G, bias=bias: s.activation(
                        out=Pr[e][G % 2][:, 0:FD], in_=Sq[e][:, 0:FD],
                        func=AF.Exp, bias=bias), sE[e])
                for e in range(2):
                    wait("vector", sE[e], G + 1)
                    emit("vector", lambda v: v.engine_nop(), sM[e])
            emit_pv(NB - 1)
            GL = G0 + NB
            wait("scalar", "rb", 64 * ridx)
            for e in range(2):
                wait("scalar", sPV[e], GL)
                sput(lambda s, e=e: s.activation(out=lnr[:, :],
                                                 in_=Ca[e][64:65, :], func=AF.Ln))
                sput(lambda s, e=e: s.activation(out=recb[e][:, :],
                                                 in_=lnr[:, :], func=AF.Exp,
                                                 scale=-1.0), inc="rc")
            wait("sync", "cv", ridx)
            for e in range(2):
                wait("sync", "rc", 2 * ridx + e + 1)
                emit("sync", lambda y, e=e: y.dma_start(
                    out=recd[e:e + 1, :], in_=recb[e][:, :]), "rb", 16)
                wait("sync", "rb", 64 * ridx + 32 * e + 16)
                emit("sync", lambda y, e=e: y.dma_start(
                    out=rbuf2[e][:, :],
                    in_=AP(recd.tensor, e * 512, [[0, 64], [1, 512]])), "rb", 16)
            for e in range(2):
                wait("vector", sPV[e], GL)
                vput(lambda v, e=e: v.tensor_copy(out=ctmp[e][:, :],
                                                  in_=Ca[e][0:64, :]), inc="ct")
            wait("vector", "ct", 2 * ridx + 2)
            wait("vector", "rb", 64 * ridx + 64)
            for e in range(2):
                vput(lambda v, e=e, p=p, qc=qc: v.tensor_tensor(
                    out=QT[p][e * 64:(e + 1) * 64, qc * 512:(qc + 1) * 512],
                    in0=ctmp[e][:, :], in1=rbuf2[e][:, :],
                    op=ALU.mult), inc="cv" if e == 1 else None)

    wait("sync", "cv", NPQ)
    for p in range(4):
        emit("sync", lambda y, p=p: y.dma_start(
            out=ctxh[p * 128:(p + 1) * 128, :], in_=QT[p][:, :]), "ctxo", 16)
    wait("gpsimd", "ctxo", 64)
    emit("gpsimd", lambda g: g.collective_compute(
        "AllGather", ALU.bypass,
        replica_groups=[[0, 1], [2, 3], [4, 5], [6, 7]],
        ins=[ctxh.opt()], outs=[ctxall.opt()]), "cc")
    emit("gpsimd", lambda g: g.dma_start(
        out=wo_sb[:, :].rearrange("p (kc n) -> p kc n", n=1024),
        in_=woi.rearrange("(kc p) n -> p kc n", p=128)), "wos", 16)
    wait("sync", "cc", 1)

    def ctx_loads(y):
        roff = y.value_load(thoff_sb[0:1, 0:1])
        for k in range(8):
            src = AP(ctxall.tensor, (k * 128) * S + roff, [[S, 128], [1, TOK]])
            y.dma_start(out=ctxT[k][:, :], in_=src).then_inc(sems["ctxi"], 16)
    SCH["sync"].append(ctx_loads)

    WQG = ([("wg", r0, 256) for r0 in range(0, FF, 256)] +
           [("wu", r0, 256) for r0 in range(0, FF, 256)] +
           [("wd", r0, 128) for r0 in range(0, D, 128)])
    NWQ = len(WQG)
    for gi, (mat, r0, n) in enumerate(WQG):
        width = D if mat in ("wg", "wu") else FF
        F = (n // 128) * width
        wait("gpsimd", "gq", gi)
        if gi == 0:
            wait("gpsimd", "pv0", NPQ * NB)
            wait("gpsimd", "pv1", NPQ * NB)
        emit("gpsimd", lambda g, mat=mat, r0=r0, n=n, F=F, width=width: g.dma_start(
            out=wstage[:, 0:F].rearrange("p (t d) -> p t d", d=width),
            in_=wsrc[mat][r0:r0 + n, :].rearrange("(t p) d -> p t d", p=128)),
            "wsts", 16)
        wait("vector", "wsts", 16 * (gi + 1))
        wait("vector", "wqs", 16 * gi)
        quant_ladder(wstage[:, 0:F], wqout[:, 0:F], F, inc="gq")
        wait("gpsimd", "gq", gi + 1)
        emit("gpsimd", lambda g, mat=mat, r0=r0, n=n, F=F, width=width: g.dma_start(
            out=wq_d[mat][r0:r0 + n, :].rearrange("(t p) d -> p t d", p=128),
            in_=wqout[:, 0:F].rearrange("p (t d) -> p t d", d=width)), "wqs", 16)

    wait("tensor", "ctxi", 128)
    wait("tensor", "wos", 16)
    wait("tensor", "cv", NPQ)
    widx = [0]
    gcnt = [0]
    dcnt = [0]
    decnt = [0]
    wtcnt = [0]
    dwcnt = [0]
    hqcnt = [0]
    xqtcnt = [0]
    lncnt = [0]
    lscnt = [0]
    for g in range(NG):
        wait("sync", "outs", 2 * g)
        emit("sync", lambda y, g=g: y.dma_start(
            out=z1g[:, :].rearrange("p (t d) -> p t d", d=D),
            in_=xres[g * GT:(g + 1) * GT, :].rearrange("(t p) d -> p t d", p=128)),
            "xrs", 16)
        wait("tensor", "dev", decnt[0])
        for tt in range(4):
            for nch2 in range(2):
                i = widx[0]
                widx[0] += 1
                ps = PS[i % 6]
                wait("tensor", "wev", i)
                for kc in range(8):
                    lhsT = ctxT[kc][:, (g * 4 + tt) * 128:(g * 4 + tt) * 128 + 128]
                    rhs = wo_sb[:, kc * 1024 + nch2 * 512:kc * 1024 + nch2 * 512 + 512]
                    emit("tensor", lambda t, l=lhsT, r=rhs, ps=ps, kc=kc: t.matmul(
                        ps, l, r, start=(kc == 0), stop=(kc == 7)),
                        "wmm" if kc == 7 else None)
                wait("vector", "wmm", i + 1)
                wait("vector", "xrs", 16 * (g + 1))
                dst = z1g[:, tt * 1024 + nch2 * 512:tt * 1024 + nch2 * 512 + 512]
                vput(lambda v, dst=dst, ps=ps: v.tensor_tensor(
                    out=dst, in0=dst, in1=ps, op=ALU.add), inc="wev")
        for tt in range(4):
            ln_stats(z1g[:, tt * 1024:(tt + 1) * 1024], tt)
        lncnt[0] += 1
        SCH["vector"].append(lambda v: v.engine_nop().then_inc(sems["lnv"], 1))
        wait("scalar", "lnv", lncnt[0])
        lscnt[0] += 1
        ln_rstd("lns")
        wait("vector", "lns", lscnt[0])
        for tt in range(4):
            dst = z1g[:, tt * 1024:(tt + 1) * 1024]
            vput(lambda v, dst=dst, tt=tt: v.tensor_scalar(
                out=dst, in0=dst, scalar1=aggr[:, 2 * tt:2 * tt + 1],
                scalar2=rstd[:, tt:tt + 1], op0=ALU.subtract, op1=ALU.mult))
        wait("vector", "xqdd", 16 * g)
        quant_ladder(z1g[:, 0:2048], xqg[:, 0:2048], 2048)
        quant_ladder(z1g[:, 2048:4096], xqg[:, 2048:4096], 2048, inc="xqs")
        wait("sync", "xqs", g + 1)
        emit("sync", lambda y: y.dma_start(
            out=xqd[:, :].rearrange("(t p) d -> p t d", p=128),
            in_=xqg[:, :].rearrange("p (t d) -> p t d", d=D)), "xqdd", 16)
        wait("sync", "xqdd", 16 * (g + 1))
        wait("sync", "gmm", gcnt[0])
        for k in range(8):
            emit("sync", lambda y, k=k: y.dma_start_transpose(
                xqT[k][:, :], xqd[:, k * 128:(k + 1) * 128]), "xqts", 16)
        xqtcnt[0] += 16 * 8
        wait("tensor", "xqts", xqtcnt[0])
        if g == 0:
            wait("sync", "wqs", 16 * NWQ)
            wait("tensor", "gev", 0)
        for ci, (n0, n) in enumerate(NCH):
            wait("sync", "gev", gcnt[0])
            wait("sync", "dmm", dcnt[0])
            for kc in range(8):
                emit("sync", lambda y, kc=kc, n0=n0, n=n: y.dma_start_transpose(
                    wtile[kc][:, 0:n],
                    wq_d["wg"][n0:n0 + n, kc * 128:(kc + 1) * 128]), "wts", 16)
                emit("sync", lambda y, kc=kc, n0=n0, n=n: y.dma_start_transpose(
                    wtile[8 + kc][:, 0:n],
                    wq_d["wu"][n0:n0 + n, kc * 128:(kc + 1) * 128]), "wts", 16)
            wtcnt[0] += 16 * 16
            wait("tensor", "wts", wtcnt[0])
            for tt in range(4):
                gps = PS[2 * (tt % 2)]
                ups = PS[2 * (tt % 2) + 1]
                wait("tensor", "gev", gcnt[0])
                for kc in range(8):
                    lhsT = xqT[kc][:, tt * 128:(tt + 1) * 128]
                    emit("tensor", lambda t, l=lhsT, kc=kc, gps=gps, n=n: t.matmul(
                        gps[:, 0:n], l, wtile[kc][:, 0:n],
                        start=(kc == 0), stop=(kc == 7)), None)
                    emit("tensor", lambda t, l=lhsT, kc=kc, ups=ups, n=n: t.matmul(
                        ups[:, 0:n], l, wtile[8 + kc][:, 0:n],
                        start=(kc == 0), stop=(kc == 7)),
                        "gmm" if kc == 7 else None)
                gcnt[0] += 1
                wait("scalar", "gmm", gcnt[0])
                wait("scalar", "gev", gcnt[0] - 1)
                emit("scalar", lambda s, gps=gps, n=n: s.activation(
                    out=gsb2[:, 0:n], in_=gps[:, 0:n], func=AF.Silu), "sil")
                wait("vector", "sil", gcnt[0])
                wait("vector", "gmm", gcnt[0])
                if ci == 0 and tt == 0:
                    wait("vector", "hqdd", 16 * g)
                vput(lambda v, gps=gps, ups=ups, n=n: v.tensor_tensor(
                    out=hf32[:, 0:n], in0=gsb2[:, 0:n], in1=ups[:, 0:n],
                    op=ALU.mult), inc="gev")
                dst = hbuf[:, tt * FF + n0:tt * FF + n0 + n]
                last_ld = (ci == len(NCH) - 1 and tt == 3)
                quant_ladder(hf32[:, 0:n], dst, n,
                             inc="hqv" if last_ld else None)
        wait("sync", "hqv", g + 1)
        emit("sync", lambda y: y.dma_start(
            out=hqd[:, :].rearrange("(t p) d -> p t d", p=128),
            in_=hbuf[:, :].rearrange("p (t d) -> p t d", d=FF)), "hqdd", 16)
        wait("sync", "hqdd", 16 * (g + 1))
        wait("sync", "dmm", dcnt[0])
        for kc in range(22):
            emit("sync", lambda y, kc=kc: y.dma_start_transpose(
                hqT[kc][:, :], hqd[:, kc * 128:(kc + 1) * 128]), "hqts", 16)
        hqcnt[0] += 16 * 22
        wait("tensor", "hqts", hqcnt[0])
        wait("sync", "gmm", gcnt[0])
        wait("tensor", "gev", gcnt[0])
        for ci2 in range(2):
            wait("tensor", "dev", decnt[0])
            for kc in range(22):
                wsl = kc % 4
                wait("sync", "dmm", dcnt[0] - 3)
                emit("sync", lambda y, kc=kc, ci2=ci2, wsl=wsl: y.dma_start_transpose(
                    wtile[wsl][:, 0:512],
                    wq_d["wd"][ci2 * 512:ci2 * 512 + 512,
                               kc * 128:(kc + 1) * 128]), "dws", 16)
                dwcnt[0] += 16
                wait("tensor", "dws", dwcnt[0])
                for tt in range(4):
                    lhsT = hqT[kc][:, tt * 128:(tt + 1) * 128]
                    emit("tensor", lambda t, l=lhsT, wsl=wsl, tt=tt, kc=kc: t.matmul(
                        PS[tt], l, wtile[wsl][:, 0:512],
                        start=(kc == 0), stop=(kc == 21)),
                        "dmm" if tt == 3 else None)
                dcnt[0] += 1
            wait("vector", "dmm", dcnt[0])
            for tt in range(4):
                dst = z1g[:, tt * 1024 + ci2 * 512:tt * 1024 + ci2 * 512 + 512]
                vput(lambda v, dst=dst, tt=tt: v.tensor_tensor(
                    out=dst, in0=dst, in1=PS[tt], op=ALU.add),
                    inc="dev" if tt == 3 else None)
            decnt[0] += 1
        for tt in range(4):
            ln_stats(z1g[:, tt * 1024:(tt + 1) * 1024], tt)
        lncnt[0] += 1
        SCH["vector"].append(lambda v: v.engine_nop().then_inc(sems["lnv"], 1))
        wait("scalar", "lnv", lncnt[0])
        lscnt[0] += 1
        ln_rstd("lns")
        wait("vector", "lns", lscnt[0])
        for half in range(2):
            wait("vector", "obar", 16 * (2 * g + half))
            for tt2 in range(2):
                tt = 2 * half + tt2
                src = z1g[:, tt * 1024:(tt + 1) * 1024]
                vput(lambda v, src=src, tt=tt, tt2=tt2: v.tensor_scalar(
                    out=outst[:, tt2 * 1024:(tt2 + 1) * 1024], in0=src,
                    scalar1=aggr[:, 2 * tt:2 * tt + 1], scalar2=rstd[:, tt:tt + 1],
                    op0=ALU.subtract, op1=ALU.mult),
                    inc="outs" if tt2 == 1 else None)
            wait("sync", "outs", 2 * g + half + 1)
            emit("sync", lambda y, g=g, half=half: y.dma_start(
                out=out[g * GT + half * 256:g * GT + (half + 1) * 256, :]
                .rearrange("(t p) d -> p t d", p=128),
                in_=outst[:, :].rearrange("p (t d) -> p t d", d=D)), "obar", 16)
    wait("sync", "obar", 16 * 2 * NG)

    with nc.Block() as block:
        @block.sync
        def _(y):
            for fn in SCH["sync"]:
                fn(y)

        @block.gpsimd
        def _(g):
            for fn in SCH["gpsimd"]:
                fn(g)

        @block.vector
        def _(v):
            for fn in SCH["vector"]:
                fn(v)

        @block.scalar
        def _(s):
            for fn in SCH["scalar"]:
                fn(s)

        @block.tensor
        def _(t):
            for fn in SCH["tensor"]:
                fn(t)

    return nc


_NC_CACHE = [None]


def kernel(**inputs):
    x = np.ascontiguousarray(np.asarray(inputs["L0_attn_x"], dtype=np.float32))
    Wq = np.ascontiguousarray(np.asarray(inputs["L0_attn_Wq"], dtype=np.float32))
    Wk = np.ascontiguousarray(np.asarray(inputs["L0_attn_Wk"], dtype=np.float32))
    Wv = np.ascontiguousarray(np.asarray(inputs["L0_attn_Wv"], dtype=np.float32))
    Wo = np.ascontiguousarray(np.asarray(inputs["L0_attn_Wo"], dtype=np.float32))
    rel = np.ascontiguousarray(np.asarray(inputs["L0_attn_rel_emb"], dtype=np.float32))
    wg = np.ascontiguousarray(np.asarray(inputs["L0_ffn_w_gate"], dtype=np.float32))
    wu = np.ascontiguousarray(np.asarray(inputs["L0_ffn_w_up"], dtype=np.float32))
    wd = np.ascontiguousarray(np.asarray(inputs["L0_ffn_w_down"], dtype=np.float32))

    if _NC_CACHE[0] is None:
        _NC_CACHE[0] = build_nc()
    nc = _NC_CACHE[0]

    oh = _onehot_host()
    o128 = np.ones((1, 128), dtype=ml_dtypes.bfloat16)
    idb = np.eye(128, dtype=np.float32).astype(ml_dtypes.bfloat16)
    in_maps = []
    for c in range(8):
        b, hh = c // 2, c % 2
        in_maps.append(dict(
            xb=x[b],
            xres=np.ascontiguousarray(x[b, hh * TOK:(hh + 1) * TOK]),
            wqi=np.ascontiguousarray(Wq[:, hh * 512:(hh + 1) * 512]),
            wki=np.ascontiguousarray(Wk[:, hh * 512:(hh + 1) * 512]),
            wvi=np.ascontiguousarray(Wv[:, hh * 512:(hh + 1) * 512]),
            woi=Wo,
            reli=np.ascontiguousarray(rel[:, hh * LH:(hh + 1) * LH]),
            wgi=wg, wui=wu, wdi=wd,
            onehot=oh, ones128=o128, identb=idb,
            thoff=np.array([[hh * TOK]], dtype=np.int32),
        ))
    res = run_bass_kernel_spmd(nc, in_maps, core_ids=list(range(8)))
    outf = np.empty((B, S, D), dtype=np.float32)
    for c in range(8):
        b, hh = c // 2, c % 2
        outf[b, hh * TOK:(hh + 1) * TOK] = np.asarray(res.results[c]["out"])
    return outf
